# version 12 (final)
# baseline (speedup 1.0000x reference)
import numpy as np
from contextlib import ExitStack
import concourse.bass as bass
import concourse.mybir as mybir
from concourse.bass_utils import run_bass_kernel_spmd

F32 = mybir.dt.float32
BF16 = mybir.dt.bfloat16
AF = mybir.ActivationFunctionType
ALU = mybir.AluOpType
AX = mybir.AxisListType

S, D = 8192, 4096
H, P, N, G, KW = 128, 64, 128, 8, 4
INTER = H * P
CONVD = INTER + 2 * G * N
PROJ = INTER + CONVD + H
EPS = 1e-5
NCORES = 8
T = S // NCORES
CH = 128
ZC = T // CH
HALF = 512
REP = H // G

ENGS = ("sync", "gpsimd", "scalar", "vector", "tensor")


class Plan:
    def __init__(self):
        self.ev = {e: [] for e in ENGS}
        self.cnt = {}

    def cur(self, sem):
        return self.cnt.get(sem, 0)

    def add(self, eng, fn, waits=(), inc=None, amt=None, hw_amt=None):
        tgt = None
        if inc is not None:
            self.cnt[inc] = self.cnt.get(inc, 0) + amt
            tgt = self.cnt[inc]
        self.ev[eng].append((tuple(waits), fn, inc, amt if hw_amt is None else hw_amt))
        return tgt


def build_nc():
    nc = bass.Bass("TRN2", target_bir_lowering=False)

    xs = nc.dram_tensor("xs", [T + 3, D], F32, kind="ExternalInput").ap()
    win = nc.dram_tensor("win", [D, PROJ], F32, kind="ExternalInput").ap()
    wout = nc.dram_tensor("wout", [INTER, D], F32, kind="ExternalInput").ap()
    wbase = nc.dram_tensor("wbase", [D, D], F32, kind="ExternalInput").ap()
    wrarr = nc.dram_tensor("wrarr", [128, 256], F32, kind="ExternalInput").ap()
    wla = nc.dram_tensor("wla", [D, 128], F32, kind="ExternalInput").ap()
    wlb = nc.dram_tensor("wlb", [128, D], F32, kind="ExternalInput").ap()
    convw = nc.dram_tensor("convw", [128, 320], F32, kind="ExternalInput").ap()
    convb = nc.dram_tensor("convb", [128, 80], F32, kind="ExternalInput").ap()
    dtbias = nc.dram_tensor("dtbias", [H], F32, kind="ExternalInput").ap()
    alog = nc.dram_tensor("alog", [H], F32, kind="ExternalInput").ap()
    normw = nc.dram_tensor("normw", [128, 64], F32, kind="ExternalInput").ap()
    cident = nc.dram_tensor("cident", [128, 128], F32, kind="ExternalInput").ap()
    ctinc = nc.dram_tensor("ctinc", [128, 128], F32, kind="ExternalInput").ap()
    cdrep = nc.dram_tensor("cdrep", [128, 128], F32, kind="ExternalInput").ap()
    cm2 = nc.dram_tensor("cm2", [128, 64], F32, kind="ExternalInput").ap()
    cmlt = nc.dram_tensor("cmlt", [128, 8], F32, kind="ExternalInput").ap()

    out = nc.dram_tensor("out", [T, D], F32, kind="ExternalOutput").ap()

    scr_hbc = nc.dram_tensor("scr_hbc", [CONVD, T], F32).ap()
    scr_gate = nc.dram_tensor("scr_gate", [T, INTER], F32).ap()
    scr_base = nc.dram_tensor("scr_base", [T, D], F32).ap()
    scr_nrmh = nc.dram_tensor("scr_nrmh", [INTER, T], BF16).ap()
    scr_nrml = nc.dram_tensor("scr_nrml", [INTER, T], BF16).ap()
    scr_cum = nc.dram_tensor("scr_cum", [H, CH], F32).ap()
    scr_row = nc.dram_tensor("scr_row", [1, 128], F32).ap()
    scr_w = nc.dram_tensor("scr_w", [8, 512], F32).ap()
    ag1_in = nc.dram_tensor("ag1_in", [128, INTER], F32)
    ag1_out = nc.dram_tensor("ag1_out", [128 * NCORES, INTER], F32)
    ag2_in = nc.dram_tensor("ag2_in", [1, 128], F32)
    ag2_out = nc.dram_tensor("ag2_out", [NCORES, 128], F32)

    pl = Plan()
    st = ExitStack()

    A = st.enter_context(nc.sbuf_tensor("poolA", [128, 32 * 515], F32))
    R = st.enter_context(nc.sbuf_tensor("poolR", [128, INTER], F32))
    C = st.enter_context(nc.sbuf_tensor("poolC", [128, 10240], F32))
    WR = st.enter_context(nc.sbuf_tensor("wring", [128, 512 * 8], F32))
    CN = st.enter_context(nc.sbuf_tensor("cnst", [128, 1024], F32))
    DT = st.enter_context(nc.sbuf_tensor("dtcm", [128, T], F32))
    MW = st.enter_context(nc.sbuf_tensor("midw", [128, T], F32))
    CWSB = st.enter_context(nc.sbuf_tensor("convw_sb", [128, 320], F32))
    NSH = st.enter_context(nc.sbuf_tensor("nsh", [128, 256], BF16))
    NSL = st.enter_context(nc.sbuf_tensor("nsl", [128, 256], BF16))
    P4NH = st.enter_context(nc.sbuf_tensor("p4nh", [128, 2048], BF16))
    P4NL = st.enter_context(nc.sbuf_tensor("p4nl", [128, 2048], BF16))
    P4WH = st.enter_context(nc.sbuf_tensor("p4wh", [128, 1024], BF16))
    P4WL = st.enter_context(nc.sbuf_tensor("p4wl", [128, 1024], BF16))
    MWH = st.enter_context(nc.sbuf_tensor("mwh", [128, 1024], BF16))
    MWL = st.enter_context(nc.sbuf_tensor("mwl", [128, 1024], BF16))
    LBH = st.enter_context(nc.sbuf_tensor("lbh", [128, 512], BF16))
    LBL = st.enter_context(nc.sbuf_tensor("lbl", [128, 512], BF16))
    SM = st.enter_context(nc.sbuf_tensor("small", [128, 1536], F32))
    PS = [st.enter_context(nc.psum_tensor(f"ps{i}", [128, 512], F32)) for i in range(8)]

    sem = {}
    for s_ in ("ls", "gs", "te", "ve", "se", "cc", "so", "t3",
               "r0", "r1", "r2", "r3", "r4", "r5", "r6", "r7",
               "hs2", "bc", "lr", "sr", "p2", "n0", "n1", "wq0", "wq1",
               "lbq", "exq", "ebq"):
        sem[s_] = st.enter_context(nc.semaphore(s_))

    block = st.enter_context(nc.Block())

    IDENT = CN[:, 0:128]
    TINC = CN[:, 128:256]
    DREP = CN[:, 256:384]
    M2 = CN[:, 384:448]
    MLT = CN[:, 448:456]
    NRMW = CN[:, 456:520]
    CONVB = CN[:, 520:600]
    DTB = CN[:, 600:601]
    ACOL = CN[:, 601:602]
    EPSC = CN[:, 602:603]
    ONEC = CN[:, 603:604]
    WRSB = CN[:, 608:864]

    dt_t = SM[:, 0:128]
    adt_cm = SM[:, 128:256]
    adt_t_s = SM[:, 256:384]
    cum_t = SM[:, 384:512]
    cum_cm = SM[:, 512:640]
    last_rep = SM[:, 640:768]
    d_t = SM[:, 768:896]
    e_rep = SM[:, 896:1024]
    et_t = SM[:, 1024:1152]
    dtacc = SM[:, 1152:1280]
    stat = SM[:, 1280:1312]
    wbuild = SM[:, 1312:1344]
    f_t = SM[:, 1344:1472]

    def dma(q, outap, inap, waits=(), s="ls", n=1):
        def fn(e, outap=outap, inap=inap):
            return e.dma_start(out=outap, in_=inap)
        return pl.add(q, fn, waits=waits, inc=s, amt=16 * n, hw_amt=16)

    def mm(outap, lhsT, rhs, start, stop, waits=()):
        def fn(e, outap=outap, lhsT=lhsT, rhs=rhs, start=start, stop=stop):
            return e.matmul(out=outap, lhsT=lhsT, rhs=rhs, start=start, stop=stop,
                            skip_group_check=True)
        return pl.add("tensor", fn, waits=waits, inc="te", amt=1)

    def tp(outap, inap, waits=()):
        p = inap.shape[0]
        def fn(e, outap=outap, inap=inap, p=p):
            return e.transpose(out=outap, in_=inap, identity=IDENT[0:p, 0:p])
        return pl.add("tensor", fn, waits=waits, inc="te", amt=1)

    def ve(fn, waits=()):
        w = list(waits) + [("ve", pl.cur("ve"))]
        return pl.add("vector", fn, waits=w, inc="ve", amt=1)

    def se(fn, waits=()):
        w = list(waits) + [("se", pl.cur("se"))]
        return pl.add("scalar", fn, waits=w, inc="se", amt=1)

    def xnc(f, j0, n):
        return A[:, f * 515 + j0: f * 515 + j0 + n]

    dma("sync", IDENT, cident[:, :])
    dma("sync", TINC, ctinc[:, :])
    dma("sync", DREP, cdrep[:, :])
    dma("sync", M2, cm2[:, :])
    dma("sync", MLT, cmlt[:, :])
    dma("sync", NRMW, normw[:, :])
    dma("sync", DTB, dtbias[:].unsqueeze(1))
    dma("sync", CWSB[:, :], convw[:, :])
    dma("sync", CONVB, convb[:, :])
    dma("sync", WRSB, wrarr[:, :])
    dma("sync", stat[:, 0:1], alog[:].unsqueeze(1))
    ld_const = pl.cur("ls")
    ve(lambda e: e.memset(EPSC, EPS))
    ve(lambda e: e.memset(ONEC, 1.0))
    se(lambda e: e.activation(out=ACOL, in_=stat[:, 0:1], func=AF.Exp),
       waits=[("ls", ld_const)])
    ve(lambda e: e.tensor_scalar(out=ACOL, in0=ACOL, scalar1=-1.0, scalar2=None, op0=ALU.mult),
       waits=[("se", pl.cur("se"))])

    widx = [0]
    ring_free_te = []

    def load_w(src_ap, ncols, extra_waits=()):
        i = widx[0]
        waits = list(extra_waits)
        if i >= 8:
            waits.append(("te", ring_free_te[i - 8]))
        slot = WR[:, (i % 8) * 512:(i % 8) * 512 + ncols]
        rs = "r%d" % (i % 8)
        tgt = dma("sync", slot, src_ap, waits=waits, s=rs)
        widx[0] += 1
        return slot, (rs, tgt)

    def wdone():
        ring_free_te.append(pl.cur("te"))

    def phase0(h):
        th = h * HALF
        starts = [(0, 128), (128, 128), (256, 128), (384, 128), (512, 3)]
        for s0, nr in starts:
            xt = C[:, 0:4096]
            sq = C[:, 4096:8192]
            qs = "t3" if nr < 128 else "ls"
            t_ld = dma("sync", xt[0:nr, :], xs[th + s0: th + s0 + nr, :],
                       waits=[("ve", pl.cur("ve")), ("te", pl.cur("te")),
                              ("gs", pl.cur("gs"))], s=qs)
            ve(lambda e, xt=xt, nr=nr: e.reduce_sum(out=stat[0:nr, 1:2], in_=xt[0:nr, :],
                                                    axis=AX.X),
               waits=[(qs, t_ld)])
            ve(lambda e, nr=nr: e.tensor_scalar(out=stat[0:nr, 2:3], in0=stat[0:nr, 1:2],
                                                scalar1=1.0 / D, scalar2=None, op0=ALU.mult))
            ve(lambda e, xt=xt, nr=nr: e.tensor_scalar(out=xt[0:nr, :], in0=xt[0:nr, :],
                                                       scalar1=stat[0:nr, 2:3],
                                                       scalar2=None, op0=ALU.subtract))
            tv = pl.cur("ve")
            se(lambda e, xt=xt, sq=sq, nr=nr: e.activation(out=sq[0:nr, :], in_=xt[0:nr, :],
                                                           func=AF.Square),
               waits=[("ve", tv)])
            ve(lambda e, sq=sq, nr=nr: e.reduce_sum(out=stat[0:nr, 3:4], in_=sq[0:nr, :],
                                                    axis=AX.X),
               waits=[("se", pl.cur("se"))])
            se(lambda e, nr=nr: e.activation(out=stat[0:nr, 4:5], in_=stat[0:nr, 3:4],
                                             func=AF.Sqrt, bias=EPSC[0:nr, :], scale=1.0 / D),
               waits=[("ve", pl.cur("ve"))])
            ve(lambda e, nr=nr: e.reciprocal(out=stat[0:nr, 4:5], in_=stat[0:nr, 4:5]),
               waits=[("se", pl.cur("se"))])
            ve(lambda e, xt=xt, nr=nr: e.tensor_scalar(out=xt[0:nr, :], in0=xt[0:nr, :],
                                                       scalar1=stat[0:nr, 4:5],
                                                       scalar2=None, op0=ALU.mult))
            tv = pl.cur("ve")
            for f in range(32):
                bank = PS[f % 2][:, 0:128]
                tp(bank[:, 0:nr], xt[0:nr, f * 128:(f + 1) * 128],
                   waits=[("ve", tv), ("se", pl.cur("se"))])
                tt = pl.cur("te")
                se(lambda e, f=f, s0=s0, nr=nr, bank=bank:
                   e.copy(out=xnc(f, s0, nr), in_=bank[:, 0:nr]),
                   waits=[("te", tt)])

    def phase1a(h):
        th = h * HALF
        for grp in range(20):
            for k in range(32):
                src = win[k * 128:(k + 1) * 128, INTER + grp * 512: INTER + (grp + 1) * 512]
                slot, wt = load_w(src, 512)
                for c in range(4):
                    lhs = slot[:, c * 128:(c + 1) * 128]
                    w0 = [wt]
                    if k == 0 and c == 0:
                        w0.append(("se", pl.cur("se")))
                        w0.append(("ve", pl.cur("ve")))
                    mm(PS[c][:, 0:512], lhs, xnc(k, 0, 512), k == 0, k == 31, waits=w0)
                    mm(PS[4 + c][:, 0:3], lhs, xnc(k, 512, 3), k == 0, k == 31)
                wdone()
            t_mm = pl.cur("te")
            for c in range(4):
                ct = grp * 4 + c
                ev = C[:, (c % 2) * 515: (c % 2) * 515 + 515]
                cv = C[:, 1030 + (c % 2) * 512: 1030 + (c % 2) * 512 + 512]
                se(lambda e, ev=ev, c=c: e.copy(out=ev[:, 0:512], in_=PS[c][:, 0:512]),
                   waits=[("te", t_mm), ("gs", pl.cur("gs")), ("ve", pl.cur("ve"))])
                se(lambda e, ev=ev, c=c: e.copy(out=ev[:, 512:515], in_=PS[4 + c][:, 0:3]))
                t_ev = pl.cur("se")
                ve(lambda e, ev=ev, cv=cv, ct=ct:
                   e.tensor_scalar(out=cv, in0=ev[:, 0:512], scalar1=CWSB[:, ct * 4:ct * 4 + 1],
                                   scalar2=None, op0=ALU.mult),
                   waits=[("se", t_ev), ("gs", pl.cur("gs"))])
                for kk in range(1, 4):
                    ve(lambda e, ev=ev, cv=cv, ct=ct, kk=kk:
                       e.scalar_tensor_tensor(out=cv, in0=ev[:, kk:kk + 512],
                                              scalar=CWSB[:, ct * 4 + kk:ct * 4 + kk + 1],
                                              in1=cv, op0=ALU.mult, op1=ALU.add))
                se(lambda e, cv=cv, ct=ct:
                   e.activation(out=cv, in_=cv, func=AF.Silu, bias=CONVB[:, ct:ct + 1], scale=1.0),
                   waits=[("ve", pl.cur("ve"))])
                dma("gpsimd", scr_hbc[ct * 128:(ct + 1) * 128, th:th + 512], cv,
                    waits=[("se", pl.cur("se"))], s="gs")
        for k in range(32):
            src = win[k * 128:(k + 1) * 128, INTER + CONVD: PROJ]
            slot, wt = load_w(src, 128)
            w0 = [wt]
            if k == 0:
                w0 += [("ve", pl.cur("ve")), ("se", pl.cur("se"))]
            mm(PS[5][:, 0:512], slot[:, 0:128], xnc(k, 3, 512), k == 0, k == 31, waits=w0)
            wdone()
        t_mm = pl.cur("te")
        a_ = C[:, 3200:3712]
        m_ = C[:, 3712:4224]
        n_ = C[:, 4224:4736]
        ve(lambda e: e.tensor_scalar(out=a_, in0=PS[5][:, 0:512], scalar1=DTB,
                                     scalar2=None, op0=ALU.add),
           waits=[("te", t_mm)])
        ve(lambda e: e.tensor_scalar(out=m_, in0=a_, scalar1=0.0, scalar2=None, op0=ALU.max))
        ve(lambda e: e.tensor_scalar(out=n_, in0=a_, scalar1=0.0, scalar2=None, op0=ALU.min))
        ve(lambda e: e.tensor_sub(out=n_, in0=n_, in1=m_))
        tv = pl.cur("ve")
        se(lambda e: e.activation(out=n_, in_=n_, func=AF.Exp), waits=[("ve", tv)])
        se(lambda e: e.activation(out=n_, in_=n_, func=AF.Ln, bias=ONEC, scale=1.0))
        ve(lambda e: e.tensor_add(out=m_, in0=m_, in1=n_), waits=[("se", pl.cur("se"))])
        ve(lambda e, th=th: e.tensor_scalar(out=DT[:, th:th + 512], in0=m_, scalar1=0.0,
                                            scalar2=100.0, op0=ALU.max, op1=ALU.min))

    def phase1b(h):
        th = h * HALF
        for t in range(4):
            for k in range(32):
                w0 = []
                if t == 0 and k == 0:
                    w0 = [("se", pl.cur("se")), ("ve", pl.cur("ve"))]
                mm(PS[6][:, t * 8:t * 8 + 8], xnc(k, 3 + t * 128, 128),
                   WRSB[:, k * 8:(k + 1) * 8], k == 0, k == 31, waits=w0)
        t_mm = pl.cur("te")
        lg = C[:, 0:32]
        pr = C[:, 32:64]
        wd = C[:, 64:96]
        v1 = stat[:, 5:6]; v2 = stat[:, 6:7]; vs = stat[:, 7:8]
        for t in range(4):
            lgt = lg[:, t * 8:t * 8 + 8]
            prt = pr[:, t * 8:t * 8 + 8]
            wdt = wd[:, t * 8:t * 8 + 8]
            ve(lambda e, lgt=lgt, t=t: e.tensor_copy(out=lgt, in_=PS[6][:, t * 8:t * 8 + 8]),
               waits=[("te", t_mm)])
            ve(lambda e, lgt=lgt: e.reduce_max(out=v1, in_=lgt, axis=AX.X))
            ve(lambda e, lgt=lgt: e.tensor_scalar(out=lgt, in0=lgt, scalar1=v1,
                                                  scalar2=None, op0=ALU.subtract))
            tv = pl.cur("ve")
            se(lambda e, lgt=lgt, prt=prt: e.activation(out=prt, in_=lgt, func=AF.Exp),
               waits=[("ve", tv)])
            ve(lambda e, prt=prt: e.reduce_sum(out=vs, in_=prt, axis=AX.X),
               waits=[("se", pl.cur("se"))])
            ve(lambda e: e.reciprocal(out=vs, in_=vs))
            ve(lambda e, prt=prt: e.tensor_scalar(out=prt, in0=prt, scalar1=vs,
                                                  scalar2=None, op0=ALU.mult))
            ve(lambda e, prt=prt: e.reduce_max(out=v1, in_=prt, axis=AX.X))
            ve(lambda e, prt=prt, wdt=wdt: e.tensor_scalar(out=wdt, in0=prt, scalar1=v1,
                                                           scalar2=None, op0=ALU.is_ge))
            ve(lambda e, prt=prt, wdt=wdt, lgt=lgt: e.tensor_mul(out=lgt, in0=prt, in1=wdt))
            ve(lambda e, prt=prt, lgt=lgt: e.tensor_sub(out=lgt, in0=prt, in1=lgt))
            ve(lambda e, lgt=lgt: e.reduce_max(out=v2, in_=lgt, axis=AX.X))
            ve(lambda e, lgt=lgt, wdt=wdt: e.scalar_tensor_tensor(
                out=wdt, in0=lgt, scalar=v2, in1=wdt, op0=ALU.is_ge, op1=ALU.add))
            ve(lambda e, prt=prt, wdt=wdt: e.tensor_mul(out=wdt, in0=wdt, in1=prt))
            ve(lambda e: e.tensor_add(out=vs, in0=v1, in1=v2))
            ve(lambda e: e.reciprocal(out=vs, in_=vs))
            ve(lambda e, wdt=wdt: e.tensor_scalar(out=wdt, in0=wdt, scalar1=vs,
                                                  scalar2=2.0, op0=ALU.mult, op1=ALU.mult))
        tv = pl.cur("ve")
        for t in range(4):
            tp(PS[7][0:8, t * 128:(t + 1) * 128], wd[:, t * 8:t * 8 + 8],
               waits=[("ve", tv), ("se", pl.cur("se"))])
        tt = pl.cur("te")
        se(lambda e: e.copy(out=C[0:8, 96:608], in_=PS[7][0:8, 0:512]), waits=[("te", tt)])
        dma("gpsimd", scr_w[:, :], C[0:8, 96:608],
            waits=[("se", pl.cur("se")), ("p2", pl.cur("p2"))], s="gs")
        t_wgs = pl.cur("gs")
        wrep = C[:, 608:1120]
        for ex in range(8):
            dma("sync", wrep[ex * 16:(ex + 1) * 16, 0:512],
                scr_w[ex:ex + 1, :].partition_broadcast(16),
                waits=[("gs", t_wgs), ("ve", pl.cur("ve"))], s="p2")
        t_wrep = pl.cur("p2")
        for k in range(32):
            slot, wt = load_w(wla[k * 128:(k + 1) * 128, :], 128)
            w0 = [wt]
            if k == 0:
                w0 += [("ve", pl.cur("ve"))]
            mm(PS[5][:, 0:512], slot[:, 0:128], xnc(k, 3, 512), k == 0, k == 31, waits=w0)
            wdone()
        t_mm = pl.cur("te")
        ve(lambda e, th=th: e.tensor_mul(out=MW[:, th:th + 512], in0=PS[5][:, 0:512], in1=wrep),
           waits=[("te", t_mm), ("p2", t_wrep)])
        for g in range(16):
            for k in range(32):
                src = win[k * 128:(k + 1) * 128, g * 512:(g + 1) * 512]
                slot, wt = load_w(src, 512)
                for t in range(4):
                    w0 = []
                    if t == 0:
                        w0 = [wt]
                        if k == 0:
                            w0 += [("se", pl.cur("se")), ("ve", pl.cur("ve"))]
                    mm(PS[t][:, 0:512], xnc(k, 3 + t * 128, 128), slot, k == 0, k == 31,
                       waits=w0)
                wdone()
            t_mm = pl.cur("te")
            for t in range(4):
                sg = C[:, 1120 + (t % 2) * 512: 1120 + (t % 2) * 512 + 512]
                se(lambda e, sg=sg, t=t: e.activation(out=sg, in_=PS[t][:, 0:512], func=AF.Silu),
                   waits=[("te", t_mm), ("gs", pl.cur("gs"))])
                dma("gpsimd", scr_gate[th + t * 128: th + (t + 1) * 128, g * 512:(g + 1) * 512],
                    sg, waits=[("se", pl.cur("se"))], s="gs")
        for o in range(8):
            for k in range(32):
                src = wbase[k * 128:(k + 1) * 128, o * 512:(o + 1) * 512]
                slot, wt = load_w(src, 512)
                for t in range(4):
                    w0 = []
                    if t == 0:
                        w0 = [wt]
                        if k == 0:
                            w0 += [("se", pl.cur("se")), ("ve", pl.cur("ve"))]
                    mm(PS[t][:, 0:512], xnc(k, 3 + t * 128, 128), slot, k == 0, k == 31,
                       waits=w0)
                wdone()
            t_mm = pl.cur("te")
            for t in range(4):
                sg = C[:, 1120 + (t % 2) * 512: 1120 + (t % 2) * 512 + 512]
                ve(lambda e, sg=sg, t=t: e.tensor_copy(out=sg, in_=PS[t][:, 0:512]),
                   waits=[("te", t_mm), ("gs", pl.cur("gs"))])
                dma("gpsimd", scr_base[th + t * 128: th + (t + 1) * 128, o * 512:(o + 1) * 512],
                    sg, waits=[("ve", pl.cur("ve"))], s="gs")

    def chunk_common(z, hs_t, hs_stage0, for_2b):
        tp(PS[6][:, 0:128], DT[:, z * CH:(z + 1) * CH],
           waits=[("ve", pl.cur("ve")), ("se", pl.cur("se")), ("gs", pl.cur("gs"))])
        se(lambda e: e.copy(out=dt_t, in_=PS[6][:, 0:128]), waits=[("te", pl.cur("te"))])
        ve(lambda e, z=z: e.tensor_scalar(out=adt_cm, in0=DT[:, z * CH:(z + 1) * CH],
                                          scalar1=ACOL, scalar2=None, op0=ALU.mult))
        tp(PS[6][:, 128:256], adt_cm, waits=[("ve", pl.cur("ve"))])
        se(lambda e: e.copy(out=adt_t_s, in_=PS[6][:, 128:256]), waits=[("te", pl.cur("te"))])
        mm(PS[6][:, 256:384], TINC, adt_t_s, True, True, waits=[("se", pl.cur("se"))])
        se(lambda e: e.copy(out=cum_t, in_=PS[6][:, 256:384]), waits=[("te", pl.cur("te"))])
        t_cum = pl.cur("se")
        t_srw = dma("gpsimd", scr_row[0:1, :], cum_t[127:128, :],
                    waits=[("se", t_cum), ("lr", pl.cur("lr"))], s="sr")
        dma("sync", last_rep, scr_row[0:1, :].partition_broadcast(128),
            waits=[("sr", t_srw)], s="lr")
        t_lr = pl.cur("lr")
        ve(lambda e: e.tensor_sub(out=d_t, in0=last_rep, in1=cum_t),
           waits=[("lr", t_lr), ("se", t_cum)])
        tv = pl.cur("ve")
        se(lambda e: e.activation(out=d_t, in_=d_t, func=AF.Exp), waits=[("ve", tv)])
        se(lambda e: e.activation(out=e_rep, in_=last_rep, func=AF.Exp))
        if for_2b:
            se(lambda e: e.activation(out=et_t, in_=cum_t, func=AF.Exp))
            tp(PS[6][:, 384:512], cum_t, waits=[("se", t_cum)])
            se(lambda e: e.copy(out=cum_cm, in_=PS[6][:, 384:512]),
               waits=[("te", pl.cur("te"))])
            dma("gpsimd", scr_cum[:, :], cum_cm,
                waits=[("se", pl.cur("se")), ("ls", pl.cur("ls"))], s="gs")
        for j in range(64):
            hsbuf = hs_stage0[:, (j % 2) * 128:(j % 2) * 128 + 128]
            hsem = "ls" if j % 2 == 0 else "hs2"
            t_ld = dma("sync", hsbuf, scr_hbc[j * 128:(j + 1) * 128, z * CH:(z + 1) * CH],
                       waits=[("te", pl.cur("te"))], s=hsem)
            bank = PS[7][:, (j % 4) * 128:(j % 4) * 128 + 128]
            tp(bank, hsbuf, waits=[(hsem, t_ld), ("se", pl.cur("se"))])
            tt = pl.cur("te")
            se(lambda e, j=j, bank=bank, hs_t=hs_t:
               e.copy(out=hs_t[:, j * 128:(j + 1) * 128], in_=bank),
               waits=[("te", tt)])

    def load_bc(z, bbuf, cbuf):
        for g in range(8):
            tb = dma("sync", bbuf[:, g * 128:(g + 1) * 128],
                     scr_hbc[INTER + g * 128: INTER + (g + 1) * 128, z * CH:(z + 1) * CH],
                     waits=[("te", pl.cur("te"))], s="bc")
        if cbuf is not None:
            for g in range(8):
                tb = dma("sync", cbuf[:, g * 128:(g + 1) * 128],
                         scr_hbc[INTER + G * N + g * 128: INTER + G * N + (g + 1) * 128,
                                 z * CH:(z + 1) * CH],
                         waits=[("te", pl.cur("te"))], s="bc")
        return tb

    def transpose_b(bbuf, bt, tgt_b):
        for g in range(8):
            bank = PS[6][:, ((g % 2) + 2) * 128:((g % 2) + 2) * 128 + 128]
            tp(bank, bbuf[:, g * 128:(g + 1) * 128],
               waits=[("bc", tgt_b), ("se", pl.cur("se"))])
            tt = pl.cur("te")
            se(lambda e, g=g, bank=bank, bt=bt: e.copy(out=bt[:, g * 128:(g + 1) * 128], in_=bank),
               waits=[("te", tt)])

    def phase2a():
        hs_t = C[:, 0:8192]
        bbuf = C[:, 8192:9216]
        hs_stage = C[:, 9216:9472]
        bt = A[:, 0:1024]
        ve(lambda e: e.memset(R[:, :], 0.0), waits=[("te", pl.cur("te")), ("gs", pl.cur("gs"))])
        ve(lambda e: e.memset(dtacc[:, :], 1.0))
        for z in range(ZC):
            chunk_common(z, hs_t, hs_stage, for_2b=False)
            tgt_b = load_bc(z, bbuf, None)
            transpose_b(bbuf, bt, tgt_b)
            ve(lambda e: e.tensor_mul(out=f_t, in0=dt_t, in1=d_t),
               waits=[("se", pl.cur("se"))])
            ve(lambda e, hs_t=hs_t: e.tensor_tensor(
                out=hs_t.rearrange("a (h p) -> a h p", h=128),
                in0=hs_t.rearrange("a (h p) -> a h p", h=128),
                in1=f_t.unsqueeze(2).broadcast_to([128, 128, 64]), op=ALU.mult))
            tv = pl.cur("ve")
            for hb in range(16):
                bank = PS[hb % 2]
                for j in range(8):
                    hh = hb * 8 + j
                    g = hh // 16
                    w0 = [("ve", pl.cur("ve"))] if j == 0 else []
                    if hb == 0 and j == 0:
                        w0.append(("ve", tv))
                    mm(bank[:, j * 64:(j + 1) * 64], bt[:, g * 128:(g + 1) * 128],
                       hs_t[:, hh * 64:(hh + 1) * 64], True, True, waits=w0)
                tt = pl.cur("te")
                sl0, sl1 = hb * 512, (hb + 1) * 512
                ve(lambda e, sl0=sl0, sl1=sl1, hb=hb: e.tensor_tensor(
                    out=R[:, sl0:sl1].rearrange("a (h p) -> a h p", h=8),
                    in0=R[:, sl0:sl1].rearrange("a (h p) -> a h p", h=8),
                    in1=e_rep[:, hb * 8:(hb + 1) * 8].unsqueeze(2).broadcast_to([128, 8, 64]),
                    op=ALU.mult),
                    waits=[("te", tt)])
                ve(lambda e, sl0=sl0, sl1=sl1, bank=bank: e.tensor_add(
                    out=R[:, sl0:sl1], in0=R[:, sl0:sl1], in1=bank[:, 0:512]))
            ve(lambda e: e.tensor_mul(out=dtacc, in0=dtacc, in1=e_rep))
        tv = pl.cur("ve")
        dma("gpsimd", ag1_in.ap()[:, :], R[:, :], waits=[("ve", tv)], s="gs")
        dma("gpsimd", ag2_in.ap()[0:1, :], dtacc[0:1, :], waits=[], s="gs")
        t_gs = pl.cur("gs")

        def cc1(e):
            return e.collective_compute("AllGather", ALU.bypass,
                                        replica_groups=[list(range(NCORES))],
                                        ins=[ag1_in.ap().opt()], outs=[ag1_out.ap().opt()])

        def cc2(e):
            return e.collective_compute("AllGather", ALU.bypass,
                                        replica_groups=[list(range(NCORES))],
                                        ins=[ag2_in.ap().opt()], outs=[ag2_out.ap().opt()])
        pl.add("gpsimd", cc1, waits=[("gs", t_gs)], inc="cc", amt=1)
        pl.add("gpsimd", cc2, waits=[], inc="cc", amt=1)

    def inc_build():
        t_cc = pl.cur("cc")
        t_ld = dma("sync", C[0:8, 0:128], ag2_out.ap()[:, :],
                   waits=[("cc", t_cc), ("te", pl.cur("te")), ("ve", pl.cur("ve"))])
        dall = wbuild[:, 0:8]
        tp(PS[6][:, 0:8], C[0:8, 0:128],
           waits=[("ls", t_ld), ("se", pl.cur("se")), ("ve", pl.cur("ve"))])
        se(lambda e: e.copy(out=dall, in_=PS[6][:, 0:8]), waits=[("te", pl.cur("te"))])
        ts_ = pl.cur("se")
        w_ = wbuild[:, 8:16]
        u_ = wbuild[:, 16:24]
        tmp = wbuild[:, 24:32]
        ve(lambda e: e.tensor_scalar(out=u_, in0=dall, scalar1=1.0, scalar2=None,
                                     op0=ALU.subtract), waits=[("se", ts_)])
        ve(lambda e: e.tensor_copy(out=w_, in_=MLT))
        for k in range(8):
            ve(lambda e, k=k: e.tensor_tensor(
                out=tmp, in0=M2[:, k * 8:(k + 1) * 8],
                in1=u_[:, k:k + 1].broadcast_to([128, 8]), op=ALU.mult))
            ve(lambda e: e.tensor_scalar(out=tmp, in0=tmp, scalar1=1.0, scalar2=None, op0=ALU.add))
            ve(lambda e: e.tensor_mul(out=w_, in0=w_, in1=tmp))
        tv = pl.cur("ve")
        tp(PS[6][0:8, 8:136], w_, waits=[("ve", tv)])
        se(lambda e: e.copy(out=C[0:8, 128:256], in_=PS[6][0:8, 8:136]),
           waits=[("te", pl.cur("te"))])
        dma("gpsimd", scr_w[:, 0:128], C[0:8, 128:256],
            waits=[("se", pl.cur("se")), ("ls", pl.cur("ls"))], s="gs")
        t_gs = pl.cur("gs")
        ve(lambda e: e.memset(R[:, :], 0.0))
        si = A[:, 0:8192]
        wrepi = A[:, 8192:8320]
        for i in range(NCORES):
            t1 = dma("sync", si, ag1_out.ap()[i * 128:(i + 1) * 128, :],
                     waits=[("ve", pl.cur("ve")), ("te", pl.cur("te"))])
            t2 = dma("sync", wrepi, scr_w[i:i + 1, 0:128].partition_broadcast(128),
                     waits=[("gs", t_gs)])
            ve(lambda e, si=si, wrepi=wrepi: e.tensor_tensor(
                out=si.rearrange("a (h p) -> a h p", h=128),
                in0=si.rearrange("a (h p) -> a h p", h=128),
                in1=wrepi.unsqueeze(2).broadcast_to([128, 128, 64]), op=ALU.mult),
                waits=[("ls", max(t1, t2))])
            ve(lambda e, si=si: e.tensor_add(out=R[:, :], in0=R[:, :], in1=si))

    def phase2b():
        hs_t = A[:, 0:8192]
        xch_t = A[:, 8192:16384]
        bbuf = C[:, 0:1024]
        cbuf = C[:, 1024:2048]
        bt = C[:, 2048:3072]
        gm = C[:, 3072:4096]
        gpp = C[:, 4096:5120]
        repb = C[:, 5120:6144]
        ygrp = C[:, 6144:7168]
        gslc = C[:, 7168:7680]
        xchd = C[:, 7680:8192]
        usl = C[:, 8192:8704]
        sqb = C[:, 8704:9728]
        nstg = C[:, 9728:9984]
        hs_stage = C[:, 9984:10240]
        for z in range(ZC):
            chunk_common(z, hs_t, hs_stage, for_2b=True)
            ve(lambda e, hs_t=hs_t, xch_t=xch_t: e.tensor_tensor(
                out=xch_t.rearrange("a (h p) -> a h p", h=128),
                in0=hs_t.rearrange("a (h p) -> a h p", h=128),
                in1=dt_t.unsqueeze(2).broadcast_to([128, 128, 64]), op=ALU.mult),
                waits=[("se", pl.cur("se"))])
            tgt_bc = load_bc(z, bbuf, cbuf)
            transpose_b(bbuf, bt, tgt_bc)
            for g in range(8):
                bank = PS[6][:, (g % 2) * 128:(g % 2) * 128 + 128]
                mm(bank, bbuf[:, g * 128:(g + 1) * 128], cbuf[:, g * 128:(g + 1) * 128],
                   True, True,
                   waits=[("bc", tgt_bc), ("ve", pl.cur("ve")),
                          ("se", pl.cur("se"))])
                tt = pl.cur("te")
                ve(lambda e, g=g, bank=bank, gm=gm:
                   e.tensor_mul(out=gm[:, g * 128:(g + 1) * 128], in0=bank, in1=TINC),
                   waits=[("te", tt)])
            t_gm_all = pl.cur("ve")
            for hb in range(16):
                g = hb // 2
                h0 = hb * 8
                pA = PS[0 + 3 * (hb % 2)]
                pB = PS[1 + 3 * (hb % 2)]
                pS = PS[2 + 3 * (hb % 2)]
                t_rep = dma("sync", repb,
                            scr_cum[h0:h0 + 8, :].rearrange("a b -> (a b)").unsqueeze(0)
                            .partition_broadcast(128),
                            waits=[("gs", pl.cur("gs")), ("ve", pl.cur("ve"))], s="p2")
                t_gld = dma("sync", gslc, scr_gate[z * CH:(z + 1) * CH, h0 * 64:(h0 + 8) * 64],
                            waits=[("ve", pl.cur("ve"))], s="p2")
                ve(lambda e, h0=h0, gpp=gpp, repb=repb: e.tensor_tensor(
                    out=gpp.rearrange("a (h l) -> a h l", h=8),
                    in0=repb.rearrange("a (h l) -> a h l", h=8),
                    in1=cum_t[:, h0:h0 + 8].unsqueeze(2).broadcast_to([128, 8, 128]),
                    op=ALU.subtract),
                    waits=[("p2", t_gld)])
                ve(lambda e, gpp=gpp: e.tensor_scalar(out=gpp, in0=gpp, scalar1=0.0,
                                                      scalar2=None, op0=ALU.min))
                tv = pl.cur("ve")
                se(lambda e, gpp=gpp: e.activation(out=gpp, in_=gpp, func=AF.Exp),
                   waits=[("ve", tv)])
                ts_ = pl.cur("se")
                ve(lambda e, g=g, gpp=gpp, gm=gm: e.tensor_tensor(
                    out=gpp.rearrange("a (h l) -> a h l", h=8),
                    in0=gpp.rearrange("a (h l) -> a h l", h=8),
                    in1=gm[:, g * 128:(g + 1) * 128].unsqueeze(1).broadcast_to([128, 8, 128]),
                    op=ALU.mult),
                    waits=[("se", ts_), ("ve", t_gm_all)])
                t_gpp = pl.cur("ve")
                ve(lambda e, h0=h0, xchd=xchd, xch_t=xch_t: e.tensor_tensor(
                    out=xchd.rearrange("a (h p) -> a h p", h=8),
                    in0=xch_t[:, h0 * 64:(h0 + 8) * 64].rearrange("a (h p) -> a h p", h=8),
                    in1=d_t[:, h0:h0 + 8].unsqueeze(2).broadcast_to([128, 8, 64]), op=ALU.mult))
                t_xchd = pl.cur("ve")
                for j in range(8):
                    hh = h0 + j
                    mm(pA[:, j * 64:(j + 1) * 64], gpp[:, j * 128:(j + 1) * 128],
                       xch_t[:, hh * 64:(hh + 1) * 64], True, True,
                       waits=[("ve", t_gpp)] if j == 0 else [])
                for j in range(8):
                    mm(pS[:, j * 64:(j + 1) * 64], bt[:, g * 128:(g + 1) * 128],
                       xchd[:, j * 64:(j + 1) * 64], True, True,
                       waits=[("ve", t_xchd)] if j == 0 else [])
                for j in range(8):
                    hh = h0 + j
                    mm(pB[:, j * 64:(j + 1) * 64], cbuf[:, g * 128:(g + 1) * 128],
                       R[:, hh * 64:(hh + 1) * 64], True, True,
                       waits=[("ve", pl.cur("ve"))] if j == 0 else [])
                t_mm = pl.cur("te")
                ve(lambda e, h0=h0, usl=usl: e.tensor_tensor(
                    out=usl.rearrange("a (h p) -> a h p", h=8),
                    in0=R[:, h0 * 64:(h0 + 8) * 64].rearrange("a (h p) -> a h p", h=8),
                    in1=e_rep[:, h0:h0 + 8].unsqueeze(2).broadcast_to([128, 8, 64]), op=ALU.mult),
                    waits=[("te", t_mm)])
                ve(lambda e, h0=h0, pS=pS, usl=usl: e.tensor_add(
                    out=R[:, h0 * 64:(h0 + 8) * 64], in0=usl, in1=pS[:, 0:512]))
                ysl = ygrp[:, (hb % 2) * 512:(hb % 2) * 512 + 512]
                ve(lambda e, h0=h0, pB=pB, ysl=ysl: e.tensor_tensor(
                    out=ysl.rearrange("a (h p) -> a h p", h=8),
                    in0=pB[:, 0:512].rearrange("a (h p) -> a h p", h=8),
                    in1=et_t[:, h0:h0 + 8].unsqueeze(2).broadcast_to([128, 8, 64]), op=ALU.mult))
                ve(lambda e, pA=pA, ysl=ysl: e.tensor_add(out=ysl, in0=ysl, in1=pA[:, 0:512]))
                ve(lambda e, h0=h0, xchd=xchd, hs_t=hs_t: e.tensor_tensor(
                    out=xchd.rearrange("a (h p) -> a h p", h=8),
                    in0=hs_t[:, h0 * 64:(h0 + 8) * 64].rearrange("a (h p) -> a h p", h=8),
                    in1=DREP[:, h0:h0 + 8].unsqueeze(2).broadcast_to([128, 8, 64]), op=ALU.mult))
                ve(lambda e, ysl=ysl, xchd=xchd: e.tensor_add(out=ysl, in0=ysl, in1=xchd))
                ve(lambda e, ysl=ysl, gslc=gslc: e.tensor_mul(out=ysl, in0=ysl, in1=gslc))
                if hb % 2 == 1:
                    gidx = hb // 2
                    tv = pl.cur("ve")
                    se(lambda e, sqb=sqb, ygrp=ygrp: e.activation(out=sqb, in_=ygrp, func=AF.Square),
                       waits=[("ve", tv)])
                    ve(lambda e, sqb=sqb: e.reduce_sum(out=stat[:, 8:9], in_=sqb, axis=AX.X),
                       waits=[("se", pl.cur("se"))])
                    se(lambda e: e.activation(out=stat[:, 9:10], in_=stat[:, 8:9],
                                              func=AF.Sqrt, bias=EPSC, scale=1.0 / 1024),
                       waits=[("ve", pl.cur("ve"))])
                    ve(lambda e: e.reciprocal(out=stat[:, 9:10], in_=stat[:, 9:10]),
                       waits=[("se", pl.cur("se"))])
                    ve(lambda e, ygrp=ygrp: e.tensor_scalar(out=ygrp, in0=ygrp,
                                                            scalar1=stat[:, 9:10],
                                                            scalar2=None, op0=ALU.mult))
                    tv = pl.cur("ve")
                    for j in range(8):
                        ct = gidx * 8 + j
                        bank = PS[7][:, (j % 2) * 128:(j % 2) * 128 + 128]
                        tp(bank, ygrp[:, j * 128:(j + 1) * 128],
                           waits=[("ve", tv), ("se", pl.cur("se"))])
                        tt = pl.cur("te")
                        stg = nstg[:, (j % 2) * 128:(j % 2) * 128 + 128]
                        sh = NSH[:, (j % 2) * 128:(j % 2) * 128 + 128]
                        sl = NSL[:, (j % 2) * 128:(j % 2) * 128 + 128]
                        se(lambda e, ct=ct, bank=bank, stg=stg:
                           e.activation(out=stg, in_=bank, func=AF.Copy,
                                        scale=NRMW[:, ct:ct + 1]),
                           waits=[("te", tt)])
                        se(lambda e, stg=stg, sh=sh: e.activation(out=sh, in_=stg,
                                                                  func=AF.Copy),
                           waits=[("gs", pl.cur("gs"))])
                        t_cast = pl.cur("se")
                        ve(lambda e, stg=stg, sh=sh, sl=sl:
                           e.tensor_sub(out=sl, in0=stg, in1=sh),
                           waits=[("se", t_cast), ("gs", pl.cur("gs"))])
                        t_sub = pl.cur("ve")
                        dma("gpsimd", scr_nrmh[ct * 128:(ct + 1) * 128, z * CH:(z + 1) * CH],
                            sh, waits=[("se", t_cast)], s="gs")
                        dma("gpsimd", scr_nrml[ct * 128:(ct + 1) * 128, z * CH:(z + 1) * CH],
                            sl, waits=[("ve", t_sub)], s="gs")

    def phase4():
        wstage = C[:, 0:1024]
        bstg = C[:, 2048:2560]
        xstg = C[:, 2560:3072]
        ostg = C[:, 3072:4096]
        lb = C[:, 4096:4608]
        te0 = pl.cur("te")
        ve0 = pl.cur("ve")
        gs0 = pl.cur("gs")
        se0 = pl.cur("se")
        nfree = [te0, te0]
        wfree = [te0, te0]
        wstage_free = [ve0, ve0]
        epil_ve = [ve0] * 8
        lb_free = te0
        lb_sub = ve0
        se(lambda e: e.activation(out=MWH[:, :], in_=MW[:, :], func=AF.Copy),
           waits=[("ve", ve0)])
        t_mwc = pl.cur("se")
        ve(lambda e: e.tensor_sub(out=MWL[:, :], in0=MW[:, :], in1=MWH[:, :]),
           waits=[("se", t_mwc)])
        t_mws = pl.cur("ve")
        for o in range(8):
            t_lb = dma("sync", lb, wlb[:, o * 512:(o + 1) * 512],
                       waits=[("te", lb_free), ("ve", lb_sub)] +
                             ([("ve", ve0), ("gs", gs0)] if o == 0 else []), s="lbq")
            se(lambda e: e.activation(out=LBH[:, :], in_=lb, func=AF.Copy),
               waits=[("lbq", t_lb), ("te", lb_free)])
            t_lbc = pl.cur("se")
            ve(lambda e: e.tensor_sub(out=LBL[:, :], in0=lb, in1=LBH[:, :]),
               waits=[("se", t_lbc), ("te", lb_free)])
            lb_sub = pl.cur("ve")
            for k in range(64):
                kk = k % 2
                nh = P4NH[:, kk * 1024:(kk + 1) * 1024]
                nl = P4NL[:, kk * 1024:(kk + 1) * 1024]
                wh = P4WH[:, kk * 512:(kk + 1) * 512]
                wl = P4WL[:, kk * 512:(kk + 1) * 512]
                ws = wstage[:, kk * 512:(kk + 1) * 512]
                ex = [("gs", gs0), ("ve", ve0)] if (o == 0 and k < 2) else []
                nsem = "n%d" % kk
                wsem = "wq%d" % kk
                dma("sync", nh, scr_nrmh[k * 128:(k + 1) * 128, :],
                    waits=[("te", nfree[kk])] + ex, s=nsem)
                t_nl = dma("sync", nl, scr_nrml[k * 128:(k + 1) * 128, :],
                           waits=[("te", nfree[kk])] + ex, s=nsem)
                t_w = dma("gpsimd", ws, wout[k * 128:(k + 1) * 128, o * 512:(o + 1) * 512],
                          waits=[("ve", wstage_free[kk])] +
                                ([("te", te0)] if (o == 0 and k < 2) else []), s=wsem)
                se(lambda e, ws=ws, wh=wh: e.activation(out=wh, in_=ws, func=AF.Copy),
                   waits=[(wsem, t_w), ("te", wfree[kk])])
                t_cast = pl.cur("se")
                ve(lambda e, ws=ws, wh=wh, wl=wl: e.tensor_sub(out=wl, in0=ws, in1=wh),
                   waits=[("se", t_cast), ("te", wfree[kk])])
                t_sub = pl.cur("ve")
                wstage_free[kk] = t_sub
                for t in range(8):
                    w0 = []
                    if t == 0:
                        w0 = [(nsem, t_nl), ("se", t_cast)]
                        if o == 0 and k == 0:
                            w0.append(("se", se0))
                    if k == 0:
                        w0.append(("ve", epil_ve[t]))
                    mm(PS[t][:, 0:512], nh[:, t * 128:(t + 1) * 128], wh,
                       k == 0, False, waits=w0)
                    mm(PS[t][:, 0:512], nh[:, t * 128:(t + 1) * 128], wl, False, False,
                       waits=[("ve", t_sub)] if t == 0 else [])
                    mm(PS[t][:, 0:512], nl[:, t * 128:(t + 1) * 128], wh, False, False)
                nfree[kk] = pl.cur("te")
                wfree[kk] = pl.cur("te")
            for t in range(8):
                w0 = []
                if t == 0:
                    w0 = [("se", t_lbc), ("ve", lb_sub)]
                    if o == 0:
                        w0.append(("ve", t_mws))
                mm(PS[t][:, 0:512], MWH[:, t * 128:(t + 1) * 128], LBH[:, :],
                   False, False, waits=w0)
                mm(PS[t][:, 0:512], MWH[:, t * 128:(t + 1) * 128], LBL[:, :],
                   False, False)
                mm(PS[t][:, 0:512], MWL[:, t * 128:(t + 1) * 128], LBH[:, :],
                   False, True)
            lb_free = pl.cur("te")
            t_mm = pl.cur("te")
            for t in range(8):
                t_b = dma("gpsimd", bstg, scr_base[t * 128:(t + 1) * 128, o * 512:(o + 1) * 512],
                          waits=[("ve", pl.cur("ve"))], s="ebq")
                t_x = dma("sync", xstg, xs[3 + t * 128: 3 + (t + 1) * 128, o * 512:(o + 1) * 512],
                          waits=[("ve", pl.cur("ve"))], s="exq")
                og = ostg[:, (t % 2) * 512:(t % 2) * 512 + 512]
                ve(lambda e, t=t, og=og, bstg=bstg: e.tensor_add(out=og, in0=PS[t][:, 0:512],
                                                                 in1=bstg),
                   waits=[("te", t_mm), ("exq", t_x), ("ebq", t_b), ("so", pl.cur("so"))])
                ve(lambda e, og=og, xstg=xstg: e.tensor_add(out=og, in0=og, in1=xstg))
                epil_ve[t] = pl.cur("ve")
                dma("gpsimd", out[t * 128:(t + 1) * 128, o * 512:(o + 1) * 512], og,
                    waits=[("ve", pl.cur("ve"))], s="so")

    for h in range(2):
        phase0(h)
        phase1a(h)
        phase1b(h)
    phase2a()
    inc_build()
    phase2b()
    phase4()

    pl.add("gpsimd", lambda e: e.memset(SM[0:1, 0:1], 0.0),
           waits=[("so", pl.cur("so"))], inc=None, amt=None)

    def emit(name, eng):
        for waits, fn, inc, amt in pl.ev[name]:
            for s_, tval in waits:
                eng.wait_ge(sem[s_], tval)
            instr = fn(eng)
            if inc is not None:
                instr.then_inc(sem[inc], amt)

    @block.sync
    def _(e):
        emit("sync", e)

    @block.gpsimd
    def _(e):
        emit("gpsimd", e)

    @block.scalar
    def _(e):
        emit("scalar", e)

    @block.vector
    def _(e):
        emit("vector", e)

    @block.tensor
    def _(e):
        emit("tensor", e)

    st.close()
    return nc


def kernel(L0_attn_x, L0_attn_in_proj_w, L0_attn_conv_w, L0_attn_conv_b,
           L0_attn_dt_bias, L0_attn_A_log, L0_attn_D, L0_attn_norm_w,
           L0_attn_out_proj_w, L0_ffn_base_W, L0_ffn_router_W,
           L0_ffn_lora_A, L0_ffn_lora_B):
    x = np.asarray(L0_attn_x, dtype=np.float32).reshape(S, D)
    win = np.ascontiguousarray(np.asarray(L0_attn_in_proj_w, np.float32).T)
    wout = np.ascontiguousarray(np.asarray(L0_attn_out_proj_w, np.float32).T)
    wbase = np.ascontiguousarray(np.asarray(L0_ffn_base_W, np.float32).T)
    wrarr = np.ascontiguousarray(
        np.asarray(L0_ffn_router_W, np.float32).T.reshape(32, 128, 8)
        .transpose(1, 0, 2).reshape(128, 256))
    wla = np.ascontiguousarray(np.transpose(np.asarray(L0_ffn_lora_A, np.float32),
                                            (1, 0, 2)).reshape(D, 128))
    wlb = np.ascontiguousarray(np.asarray(L0_ffn_lora_B, np.float32).reshape(128, D))
    convw = np.ascontiguousarray(
        np.asarray(L0_attn_conv_w, np.float32).reshape(80, 128, KW)
        .transpose(1, 0, 2).reshape(128, 320))
    convb = np.ascontiguousarray(np.asarray(L0_attn_conv_b, np.float32)
                                 .reshape(80, 128).T)
    dtbias = np.asarray(L0_attn_dt_bias, np.float32)
    alog = np.asarray(L0_attn_A_log, np.float32)
    dvec = np.asarray(L0_attn_D, np.float32)
    normw = np.ascontiguousarray(np.asarray(L0_attn_norm_w, np.float32)
                                 .reshape(64, 128).T)

    cident = np.eye(128, dtype=np.float32)
    ctinc = np.triu(np.ones((128, 128), np.float32))
    cdrep = np.ascontiguousarray(np.broadcast_to(dvec, (128, 128)))

    nc = build_nc()
    in_maps = []
    for c in range(NCORES):
        t0 = c * T
        if c == 0:
            xsl = np.concatenate([np.zeros((3, D), np.float32), x[0:T, :]], 0)
        else:
            xsl = x[t0 - 3: t0 + T, :]
        m2 = np.zeros((8, 8), np.float32)
        for k in range(8):
            for i in range(8):
                if i < k < c:
                    m2[k, i] = 1.0
        cm2 = np.ascontiguousarray(np.broadcast_to(m2.reshape(1, 64), (128, 64)))
        mlt = np.zeros((8,), np.float32)
        mlt[:c] = 1.0
        cmlt = np.ascontiguousarray(np.broadcast_to(mlt, (128, 8)))
        in_maps.append({
            "xs": np.ascontiguousarray(xsl), "win": win, "wout": wout, "wbase": wbase,
            "wrarr": wrarr, "wla": wla, "wlb": wlb, "convw": convw, "convb": convb,
            "dtbias": dtbias, "alog": alog, "normw": normw,
            "cident": cident, "ctinc": ctinc, "cdrep": cdrep, "cm2": cm2, "cmlt": cmlt,
        })
    res = run_bass_kernel_spmd(nc, in_maps, core_ids=list(range(NCORES)))
    outp = np.concatenate([np.asarray(res.results[c]["out"]) for c in range(NCORES)], 0)
    return outp.reshape(1, S, D)


if __name__ == "__main__":
    import reference as refmod
    inp = refmod.setup_inputs()
    outk = kernel(**{k: np.asarray(v) for k, v in inp.items()})
    ref = np.asarray(refmod.reference(**inp))
    err = np.abs(outk - ref).max()
    rel = err / np.abs(ref).max()
    print("max_abs", err, "rel", rel)
